# version 04
# baseline (speedup 1.0000x reference)
import numpy as np
import ml_dtypes
import concourse.bass as bass
import concourse.mybir as mybir
from concourse.bass_utils import run_bass_kernel_spmd
from contextlib import ExitStack

F32 = mybir.dt.float32
BF16 = mybir.dt.bfloat16
AF = mybir.ActivationFunctionType
ALU = mybir.AluOpType
AX = mybir.AxisListType.X

D, DI, HF, NS = 1024, 2048, 4096, 16
NTOK = 4096
NH, TH = 8, 512
NQ, QT = 16, 256
KD, KC, KH = 8, 16, 32
EPS = 1e-5
RSQ2 = 0.7071067811865476

ENGS = ("sync", "tensor", "vector", "scalar", "gpsimd")
DEBUG = False


class Sem:
    def __init__(self, nc, es, name):
        self.h = es.enter_context(nc.semaphore(name))
        self.n = 0


def build_program(debug=False):
    nc = bass.Bass("TRN2", target_bir_lowering=False)

    def di(name, shape, dt=F32):
        return nc.dram_tensor(name, shape, dt, kind="ExternalInput").ap()

    x_in = di("x", [NTOK, D])
    xprev_in = di("xprev", [128, D])
    hmask_in = di("hmask", [128, 1])
    sel_in = di("sel", [8, 1])
    cst_in = di("cst", [128, 8])
    idf_in = di("idf", [128, 128])
    win_in = di("W_in", [D, 2 * DI])
    cw_in = di("cwT", [128, 64])
    cb_in = di("cbT", [128, KC])
    alog_in = di("A_log", [NS, 1])
    bmat_in = di("B_mat", [NS, DI])
    cmat_in = di("C_mat", [DI, NS])
    dv_in = di("DvT", [128, KC])
    wd_in = di("Wd", [DI, DI])
    bd_in = di("bdT", [128, KC])
    wo_in = di("W_out", [DI, D])
    gam_in = di("gamT", [128, KD])
    bet_in = di("betT", [128, KD])
    w1_in = di("w1", [HF, D])
    b1_in = di("b1T", [128, KH])
    w2_in = di("w2", [D, HF])
    b2bc_in = di("b2bc", [128, D])

    out_d = nc.dram_tensor("out", [NTOK, D], F32, kind="ExternalOutput").ap()

    win_bf = nc.dram_tensor("win_bf", [D, 2 * DI], BF16, kind="Internal")
    wd_bf = nc.dram_tensor("wd_bf", [DI, DI], BF16, kind="Internal")
    wo_bf = nc.dram_tensor("wo_bf", [DI, D], BF16, kind="Internal")
    s1t_bf = nc.dram_tensor("s1t_bf", [D, HF], BF16, kind="Internal")
    s2t_bf = nc.dram_tensor("s2t_bf", [HF, D], BF16, kind="Internal")
    xs2_sc = nc.dram_tensor("xs2_sc", [DI, NTOK], F32, kind="Internal")
    gate_sc = nc.dram_tensor("gate_sc", [DI, NTOK], F32, kind="Internal")
    z3_sc = nc.dram_tensor("z3_sc", [D, NTOK], BF16, kind="Internal")
    x1_sc = nc.dram_tensor("x1_sc", [NTOK, D], F32, kind="Internal")
    cc_i = nc.dram_tensor("cc_i", [1, NS], F32, kind="Internal")
    cc_o = nc.dram_tensor("cc_o", [8, NS], F32, kind="Internal")
    dbg = {}
    if True:
        for nm, shp in (("d_da", [NS, NTOK]), ("d_bx", [NS, NTOK]),
                        ("d_hs", [NS, NTOK]), ("d_x1", [NTOK, D]),
                        ("d_delta", [1, NTOK]), ("d_xs2", [DI, NTOK]),
                        ("d_gate", [DI, NTOK]), ("d_sp", [128, NTOK]),
                        ("d_z", [128, 4 * D]), ("d_h", [128, KD * TH]),
                        ("d_xh", [128, 4 * D]), ("d_mv", [128, 32]),
                        ("d_sl", [128, 4 * TH]), ("d_s1", [D, HF]),
                        ("d_s2", [HF, D]), ("d_z3", [D, NTOK]),
                        ("d_a", [128, 3 * KH + 2 * D])):
            dbg[nm] = nc.dram_tensor(nm, shp, F32, kind="ExternalOutput").ap()

    es = ExitStack()
    with es:
        def sb(name, shape, dt=F32):
            return es.enter_context(nc.sbuf_tensor("sb_" + name, shape, dt))

        cst = sb("cst", [128, 8])
        idf = sb("idf", [128, 128])
        idb = sb("idb", [128, 128], BF16)
        onesb = sb("onesb", [128, 1], BF16)
        ones116 = sb("ones116", [1, 16])
        onesf1 = sb("onesf1", [1, 128])
        gam_f = sb("gam_f", [128, KD])
        bet_f = sb("bet_f", [128, KD])
        cwv = sb("cwv", [128, 64])
        cbv = sb("cbv", [128, KC])
        dvv = sb("dvv", [128, KC])
        bdv = sb("bdv", [128, KC])
        b1v = sb("b1v", [128, KH])
        a1v = sb("a1v", [128, KH])
        a1s = sb("a1s", [128, KH])
        b1s = sb("b1s", [128, KH])
        a2bc = sb("a2bc", [128, D])
        b2bc = sb("b2bc", [128, D])
        acol = sb("acol", [NS, 1])
        atmp = sb("atmp", [NS, 1])
        BT = sb("BT", [128, KC * NS], BF16)
        CT = sb("CT", [NS, DI], BF16)
        selv = sb("selv", [8, 1])
        hmv = sb("hmv", [128, 1])
        ps1a = sb("ps1a", [128, 32])
        pb1a = sb("pb1a", [128, 32])
        delta_r = sb("delta_r", [1, TH])
        d16t = sb("d16t", [NS, TH])
        bxt = sb("bxt", [NS, TH])
        dAt = sb("dAt", [NS, NTOK])
        Bxt = sb("Bxt", [NS, NTOK])
        halo = sb("halo", [128, 3 * KC])
        htmp3 = sb("htmp3", [128, 3])
        hrow = sb("hrow", [1, NS])
        Gsb = sb("Gsb", [8, NS])
        h0sb = sb("h0sb", [NS, 1])
        alcol = sb("alcol", [128, 1])
        alrow = sb("alrow", [1, 8 * 128])
        o16c = sb("o16c", [NS, 128])
        altmp = sb("altmp", [128, 4])
        a2t16 = sb("a2t16", [NS, 128])
        bnst = sb("bnst", [128, 12 * 4])
        mvt = sb("mvt", [128, 2 * 4])
        lv1 = sb("lv1", [128, 4])
        r2t = sb("r2t", [128, 4])
        vzt = sb("vzt", [128, 4])
        lv2 = sb("lv2", [128, 4])
        s12v = sb("s12v", [128, 4])
        rs12t = sb("rs12t", [128, 4])
        rs1t = sb("rs1t", [128, 4])
        nb12t = sb("nb12t", [128, 4])
        muc = sb("muc", [128, 4])
        varc = sb("varc", [128, 4])
        snap = sb("snap", [128, 64])
        snap2 = sb("snap2", [128, 64])
        snap3 = sb("snap3", [128, 64])
        snap4 = sb("snap4", [128, 192])

        nc.const_aps.register(F32, EPS, cst[:, 2:3])

        arena_base = (int(nc.sbuf_base) + 255) // 256 * 256
        arena_top = int(nc.sbuf_top) - 256
        cur = [arena_base]
        uid = [0]

        def at(name, shape, dt=F32):
            nbytes = int(np.prod(shape[1:])) * (2 if dt == BF16 else 4)
            nbytes = (nbytes + 31) // 32 * 32
            off = cur[0]
            assert off + nbytes <= arena_top, (
                f"arena overflow {name}: need {nbytes}B at {off}, top {arena_top}")
            uid[0] += 1
            h = nc.alloc_sbuf_tensor_at(f"{name}_{uid[0]}", shape, dt,
                                        offset=off)
            cur[0] = off + nbytes
            return h

        PW = [es.enter_context(nc.psum_tensor(f"PW{i}", [128, 1024], F32))
              for i in range(2)]
        PD = es.enter_context(nc.psum_tensor("PD", [1, 512], F32))
        PS = es.enter_context(nc.psum_tensor("PS", [NS, 1024], F32))

        prog = {e: [] for e in ENGS}
        semnames = ["s_ld", "s_win", "s_wd", "s_wst", "s_wb", "s_x3",
                    "s_rr", "s_out", "s_pe", "s_sc", "s_v", "s_g", "s_cc"]
        for tag in ("b1", "b2", "b3", "b4"):
            semnames += [f"{tag}{c}" for c in "ytvsg"]
        S = {n: Sem(nc, es, n) for n in semnames}

        def W(e, sem, v):
            assert v is not None and v > 0, f"bad wait {e}"
            prog[e].append(lambda E, s=sem.h, v=v: E.wait_ge(s, v))

        def EM(e, f, sem=None, k=1):
            if sem is None:
                prog[e].append(f)
                return None
            sem.n += k
            v = sem.n
            prog[e].append(lambda E, f=f, s=sem.h, k=k: f(E).then_inc(s, k))
            return v

        def NOP(e, cyc):
            prog[e].append(lambda E, c=cyc: E.nop(cycle_cnt=c))

        def load_act_set(idx):
            def f(E):
                E.add_instruction(mybir.InstLoadActFuncSet(
                    name=nc.get_next_instruction_name(),
                    act_func_set_id=idx, ins=[], outs=[]))
            prog["scalar"].append(f)
            NOP("scalar", 30000)

        def barrier(tag, real=None):
            real = real or {}
            bs = dict(zip(ENGS, (S[f"{tag}{c}"] for c in "ytvsg")))
            for e in ("vector", "scalar", "gpsimd"):
                if e not in real:
                    NOP(e, 6000)
            vals = {e: EM(e, lambda E: E.nop(), bs[e]) for e in ENGS}
            for e in ENGS:
                for e2 in ENGS:
                    if e2 == e:
                        continue
                    if e2 in real:
                        sem, v = real[e2]
                        W(e, sem, v)
                    else:
                        W(e, bs[e2], vals[e2])

        sld, swin, swd = S["s_ld"], S["s_win"], S["s_wd"]
        swst, swb, sx3, srr = S["s_wst"], S["s_wb"], S["s_x3"], S["s_rr"]
        sout, spe, ssc, sv = S["s_out"], S["s_pe"], S["s_sc"], S["s_v"]
        sg, scc = S["s_g"], S["s_cc"]

        for dst, src in ((cst, cst_in), (idf, idf_in), (selv, sel_in),
                         (hmv, hmask_in), (gam_f, gam_in), (bet_f, bet_in),
                         (cwv, cw_in), (cbv, cb_in), (dvv, dv_in),
                         (bdv, bd_in), (b1v, b1_in), (b2bc, b2bc_in),
                         (atmp, alog_in)):
            EM("sync", lambda E, d=dst, s=src: E.dma_start(out=d[:, :], in_=s),
               sld, 16)
        v_ld0 = sld.n

        wtmp = [at(f"wtmp{i}", [128, HF]) for i in range(2)]
        wtmpb = [at(f"wtmpb{i}", [128, HF], BF16) for i in range(2)]
        rowb = at("rowb", [128, KD * HF], BF16)
        b16 = at("b16", [NS, DI])
        ctmp = at("ctmp", [128, KC * NS])

        EM("sync", lambda E: E.dma_start(out=b16[:, :], in_=bmat_in), sld, 16)
        for c in range(KC):
            EM("sync", lambda E, c=c: E.dma_start(
                out=ctmp[:, c * NS:(c + 1) * NS],
                in_=cmat_in[c * 128:(c + 1) * 128, :]), sld, 16)
        v_ldbc = sld.n

        W("vector", sld, v_ld0)
        EM("vector", lambda E: E.tensor_copy(idb[:, :], idf[:, :]))
        EM("vector", lambda E: E.memset(onesb[:, :], 1.0))
        EM("vector", lambda E: E.memset(ones116[:, :], 1.0))
        EM("vector", lambda E: E.memset(onesf1[:, :], 1.0))
        EM("vector", lambda E: E.memset(o16c[:, :], 1.0 / NS))
        EM("vector", lambda E: E.tensor_scalar(
            b1s[:, :], b1v[:, :], RSQ2, None, ALU.mult))
        W("vector", sld, v_ldbc)
        v_vbase = EM("vector", lambda E: E.nop(), sv)

        load_act_set(6)
        W("scalar", sld, v_ld0)
        v_ae = EM("scalar", lambda E: E.activation(acol[:, :], atmp[:, :],
                                                   AF.Exp), ssc)
        W("vector", ssc, v_ae)
        EM("vector", lambda E: E.tensor_scalar(
            acol[:, :], acol[:, :], -1.0, None, ALU.mult), sv)

        W("tensor", sv, v_vbase)
        for k in range(KC):
            sl = (k % 4) * NS
            EM("tensor", lambda E, k=k, sl=sl: E.transpose(
                PW[0][:, sl:sl + NS], b16[0:NS, k * 128:(k + 1) * 128],
                idf[0:NS, 0:NS]), spe if k % 4 == 3 else None)
            if k % 4 == 3:
                W("scalar", spe, spe.n)
                vs = EM("scalar", lambda E, k=k: E.copy(
                    BT[:, (k - 3) * NS:(k + 1) * NS], PW[0][:, 0:4 * NS]), ssc)
                W("tensor", ssc, vs)
        for c in range(KC):
            sl = (c % 8) * 128
            EM("tensor", lambda E, c=c, sl=sl: E.transpose(
                PS[0:NS, sl:sl + 128], ctmp[:, c * NS:(c + 1) * NS],
                idf[:, :]), spe if c % 8 == 7 else None)
            if c % 8 == 7:
                W("scalar", spe, spe.n)
                vs = EM("scalar", lambda E, c=c: E.copy(
                    CT[:, (c - 7) * 128:(c + 1) * 128], PS[0:NS, 0:1024]), ssc)
                W("tensor", ssc, vs)

        wtmp_free = []
        wtmpb_free = []
        jn = [0]

        def job_load(src_ap, row0, cols):
            j = jn[0]
            if j >= 2:
                sem, v = wtmp_free[j - 2]
                W("sync", sem, v)
            EM("sync", lambda E, s=src_ap, r=row0, c=cols, p=j % 2:
               E.dma_start(out=wtmp[p][:, 0:c], in_=s[r:r + 128, :]), sld, 16)
            return sld.n, j % 2

        prep_end = cur[0]
        cur[0] = arena_base
        winb = [at(f"winb{i}", [128, KD * 128], BF16) for i in range(2)]
        wdb = [at(f"wdb{i}", [128, KC * 128], BF16) for i in range(2)]
        x_half = at("x_half", [128, 4 * D])
        zhalf = at("zhalf", [128, 4 * D], BF16)
        h_fm = at("h_fm", [128, KD * TH], BF16)
        xs2b = at("xs2b", [128, KC * TH], BF16)
        xse = [at(f"xse{i}", [128, TH + 3]) for i in range(2)]
        cv0 = [at(f"cv0{i}", [128, TH]) for i in range(2)]
        cv1 = [at(f"cv1{i}", [128, TH]) for i in range(2)]
        se_ = [at(f"se{i}", [128, TH]) for i in range(2)]
        xs2f = [at(f"xs2f{i}", [128, TH]) for i in range(2)]
        gatef = [at(f"gatef{i}", [128, TH]) for i in range(2)]
        spe_ = [at(f"spe{i}", [128, TH]) for i in range(2)]
        spb = [at(f"spb{i}", [128, TH], BF16) for i in range(2)]
        hp_fm = at("hp_fm", [128, KD * 128], BF16)
        assert cur[0] <= arena_top

        def cast_bounce(src_ap, dst_dram, rows, cols):
            for k in range(rows // 128):
                vl, par = job_load(src_ap, k * 128, cols)
                W("vector", sld, vl)
                if len(wtmpb_free) >= 2:
                    sem, v = wtmpb_free[-2]
                    W("vector", sem, v)
                m = EM("vector", lambda E, par=par, c=cols: E.tensor_copy(
                    wtmpb[par][:, 0:c], wtmp[par][:, 0:c]), sv)
                wtmp_free.append((sv, m))
                W("sync", sv, m)
                ms = EM("sync", lambda E, k=k, par=par, c=cols, d=dst_dram:
                        E.dma_start(out=d.ap()[k * 128:(k + 1) * 128, :],
                                    in_=wtmpb[par][:, 0:c]), swst, 16)
                wtmpb_free.append((swst, ms))
                jn[0] += 1

        cast_bounce(win_in, win_bf, D, 2 * DI)
        cast_bounce(wd_in, wd_bf, DI, DI)
        cast_bounce(wo_in, wo_bf, DI, D)

        def binarize(src_ap, dst_dram, rows, cols, mode):
            nck, nj, roww = rows // 128, cols // 128, rows
            rview = rowb[:, 0:nj * roww].rearrange("p (j f) -> p j f", f=roww)
            for c in range(nck):
                vl, par = job_load(src_ap, c * 128, cols)
                W("vector", sld, vl)
                nred = (cols + 1023) // 1024
                for rr in range(nred):
                    EM("vector", lambda E, par=par, rr=rr: E.tensor_reduce(
                        altmp[:, rr:rr + 1],
                        wtmp[par][:, rr * 1024:(rr + 1) * 1024], AX, ALU.add,
                        apply_absolute_value=True))
                NOP("vector", 2500)
                if nred == 1:
                    EM("vector", lambda E: E.tensor_copy(
                        alcol[:, :], altmp[:, 0:1]))
                else:
                    EM("vector", lambda E: E.tensor_add(
                        altmp[:, 0:1], altmp[:, 0:1], altmp[:, 1:2]))
                    EM("vector", lambda E: E.tensor_add(
                        altmp[:, 2:3], altmp[:, 2:3], altmp[:, 3:4]))
                    NOP("vector", 2000)
                    EM("vector", lambda E: E.tensor_add(
                        alcol[:, :], altmp[:, 0:1], altmp[:, 2:3]))
                NOP("vector", 2500)
                if mode == "w1":
                    EM("vector", lambda E, c=c, co=cols: E.tensor_scalar(
                        a1v[:, c:c + 1], alcol[:, :], 1.0 / co, None,
                        ALU.mult))
                    vv = EM("vector", lambda E, c=c, co=cols: E.tensor_scalar(
                        a1s[:, c:c + 1], alcol[:, :], RSQ2 / co, None,
                        ALU.mult), sv)
                else:
                    vv = EM("vector", lambda E, co=cols: E.tensor_scalar(
                        alcol[:, :], alcol[:, :], 1.0 / co, None, ALU.mult),
                        sv)
                if len(wtmpb_free) >= 2:
                    sem, v = wtmpb_free[-2]
                    W("scalar", sem, v)
                W("scalar", sv, vv)
                msg = EM("scalar", lambda E, par=par, co=cols: E.activation(
                    wtmpb[par][:, 0:co], wtmp[par][:, 0:co], AF.Sign), ssc)
                wtmp_free.append((ssc, msg))
                if mode == "w2":
                    W("tensor", sv, vv)
                    sl = 512 + (c % 4) * 128
                    vt = EM("tensor", lambda E, sl=sl: E.transpose(
                        PS[0:1, sl:sl + 128], alcol[:, :], idf[:, :]), spe)
                    W("scalar", spe, vt)
                    vs2 = EM("scalar", lambda E, c=c, sl=sl: E.copy(
                        alrow[:, c * 128:(c + 1) * 128], PS[0:1, sl:sl + 128]),
                        ssc)
                    W("tensor", ssc, vs2)
                W("tensor", ssc, msg)
                jend = None
                for j in range(nj):
                    pj = j % 8
                    last = (pj == 7 or j == nj - 1)
                    EM("tensor", lambda E, j=j, pj=pj, par=par: E.matmul(
                        PW[par][:, pj * 128:(pj + 1) * 128],
                        wtmpb[par][:, j * 128:(j + 1) * 128], idb[:, :],
                        start=True, stop=True), spe if last else None)
                    if last:
                        W("scalar", spe, spe.n)
                        jend = EM("scalar",
                                  lambda E, c=c, par=par, j0=j - pj, pj=pj:
                                  E.copy(rview[:, j0:j0 + pj + 1,
                                               c * 128:(c + 1) * 128],
                                         PW[par][:, 0:(pj + 1) * 128]), ssc)
                        W("tensor", ssc, jend)
                wtmpb_free.append((ssc, jend))
                jn[0] += 1
            W("sync", ssc, wtmpb_free[-1][1])
            for j in range(nj):
                EM("sync", lambda E, j=j, roww=roww, d=dst_dram: E.dma_start(
                    out=d.ap()[j * 128:(j + 1) * 128, :],
                    in_=rowb[:, j * roww:(j + 1) * roww]), swst, 16)
            if mode == "w2":
                W("tensor", ssc, ssc.n)
                for c in range(nck):
                    EM("tensor", lambda E, c=c: E.matmul(
                        PS[0:NS, 0:128], ones116[:, :],
                        alrow[:, c * 128:(c + 1) * 128],
                        start=True, stop=True), spe)
                    W("scalar", spe, spe.n)
                    v16e = EM("scalar", lambda E: E.copy(
                        a2t16[:, :], PS[0:NS, 0:128]), ssc)
                    W("tensor", ssc, v16e)
                    EM("tensor", lambda E, c=c: E.matmul(
                        PW[c % 2][:, 0:128], o16c[:, :], a2t16[:, :],
                        start=True, stop=True), spe)
                    W("scalar", spe, spe.n)
                    va = EM("scalar", lambda E, c=c: E.copy(
                        a2bc[:, c * 128:(c + 1) * 128], PW[c % 2][:, 0:128]),
                        ssc)
                    W("tensor", ssc, va)

        binarize(w1_in, s1t_bf, HF, D, "w1")
        binarize(w2_in, s2t_bf, D, HF, "w2")

        barrier("b1")
        W("sync", swst, swst.n)

        pw_free = [None, None]

        def emit_stats(xbuf, ngr, goff, ln2):
            for g in range(ngr):
                for hh in range(2):
                    EM("vector", lambda E, g=g, hh=hh, xb=xbuf: E.bn_stats(
                        bnst[:, g * 12 + hh * 6:g * 12 + hh * 6 + 6],
                        xb[:, g * D + hh * 512:g * D + (hh + 1) * 512]))
            NOP("vector", 500)
            for g in range(ngr):
                EM("vector", lambda E, g=g: E.bn_aggr(
                    mvt[:, g * 2:g * 2 + 2], bnst[:, g * 12:g * 12 + 12]),
                   sv if g == ngr - 1 else None)
            NOP("vector", 3000)
            for g in range(ngr):
                EM("vector", lambda E, g=g: E.tensor_copy(
                    muc[:, g:g + 1], mvt[:, 2 * g:2 * g + 1]))
            for g in range(ngr):
                EM("vector", lambda E, g=g: E.tensor_copy(
                    varc[:, g:g + 1], mvt[:, 2 * g + 1:2 * g + 2]),
                   sv if g == ngr - 1 else None)
            v_mv = sv.n
            var_v = varc[:, 0:ngr]
            mu_v = muc[:, 0:ngr]
            W("scalar", sv, v_mv)
            EM("scalar", lambda E, n=ngr, vv=var_v: E.activation(
                lv1[:, 0:n], vv, AF.Ln, bias=cst[:, 2:3]))
            NOP("scalar", 1500)
            if ln2:
                EM("scalar", lambda E, n=ngr: E.activation(
                    r2t[:, 0:n], lv1[:, 0:n], AF.Exp, scale=-1.0))
                v_s1 = EM("scalar", lambda E, n=ngr: E.activation(
                    rs1t[:, 0:n], lv1[:, 0:n], AF.Exp, scale=-0.5), ssc)
                W("vector", ssc, v_s1)
                v_vz = EM("vector", lambda E, n=ngr, vv=var_v: E.tensor_mul(
                    vzt[:, 0:n], vv, r2t[:, 0:n]), sv)
                W("scalar", sv, v_vz)
                NOP("scalar", 1000)
                v_l2 = EM("scalar", lambda E, n=ngr: E.activation(
                    lv2[:, 0:n], vzt[:, 0:n], AF.Ln, bias=cst[:, 2:3]), ssc)
                W("vector", ssc, v_l2)
                v_s12 = EM("vector", lambda E, n=ngr: E.tensor_add(
                    s12v[:, 0:n], lv1[:, 0:n], lv2[:, 0:n]), sv)
                W("scalar", sv, v_s12)
                NOP("scalar", 1000)
                v_r12 = EM("scalar", lambda E, n=ngr: E.activation(
                    rs12t[:, 0:n], s12v[:, 0:n], AF.Exp, scale=-0.5), ssc)
            else:
                NOP("scalar", 1500)
                v_r12 = EM("scalar", lambda E, n=ngr: E.activation(
                    rs12t[:, 0:n], lv1[:, 0:n], AF.Exp, scale=-0.5), ssc)
            W("vector", ssc, v_r12)
            NOP("vector", 300)
            EM("vector", lambda E, n=ngr, mv=mu_v: E.scalar_tensor_tensor(
                nb12t[:, 0:n], mv, -1.0, rs12t[:, 0:n], ALU.mult, ALU.mult))
            if goff is not None:
                EM("vector", lambda E, n=ngr, go=goff: E.tensor_scalar(
                    ps1a[:, go:go + n], rs1t[:, 0:n], 1.0, 1.0,
                    ALU.mult, ALU.add))
                EM("vector", lambda E, n=ngr, go=goff, mv=mu_v:
                   E.scalar_tensor_tensor(
                       pb1a[:, go:go + n], mv, -1.0, rs1t[:, 0:n],
                       ALU.mult, ALU.mult))
            NOP("vector", 500)

        def emit_z(xbuf, zdst, ngr):
            for g in range(ngr):
                EM("vector", lambda E, g=g, xb=xbuf, zd=zdst: E.tensor_scalar(
                    zd[:, g * D:(g + 1) * D], xb[:, g * D:(g + 1) * D],
                    rs12t[:, g:g + 1], nb12t[:, g:g + 1], ALU.mult, ALU.add),
                   sv if g == ngr - 1 else None)
            return sv.n

        def emit_transposes(zsrc, hdst, ngr, gb, affine):
            v_z = sv.n
            W("tensor", sv, v_z)
            for j in range(KD):
                par = j % 2
                if pw_free[par] is not None:
                    W("tensor", ssc, pw_free[par])
                for g in range(ngr):
                    EM("tensor", lambda E, j=j, g=g, par=par, z=zsrc:
                       E.matmul(PW[par][:, g * 128:(g + 1) * 128],
                                z[:, g * D + j * 128:g * D + (j + 1) * 128],
                                idb[:, :], start=True, stop=True),
                       spe if g == ngr - 1 else None)
                W("scalar", spe, spe.n)
                if affine:
                    pw_free[par] = EM(
                        "scalar", lambda E, j=j, par=par, h=hdst, gb=gb:
                        E.activation(h[:, j * gb:(j + 1) * gb],
                                     PW[par][:, 0:gb], AF.Identity,
                                     scale=gam_f[:, j:j + 1],
                                     bias=bet_f[:, j:j + 1]), ssc)
                else:
                    pw_free[par] = EM(
                        "scalar", lambda E, j=j, par=par, h=hdst, gb=gb:
                        E.activation(h[:, j * gb:(j + 1) * gb],
                                     PW[par][:, 0:gb], AF.Identity), ssc)

        xse_free = [None, None]
        cv1_free = [None, None]
        se_free = [None, None]
        xs2f_free = [None, None]
        gatef_free = [None, None]
        spb_free = [None, None]
        wdb_free = [None, None]
        xs2b_guard = [None]

        def emit_silu(hI, m, par, v_cv):
            W("scalar", sv, v_cv)
            if se_free[par] is not None:
                W("scalar", sv, se_free[par])
            v_e = EM("scalar", lambda E, par=par: E.activation(
                se_[par][:, :], cv1[par][:, :], AF.Exp, scale=-1.0), ssc)
            cv1_free[par] = v_e
            W("vector", ssc, v_e)
            EM("vector", lambda E, par=par: E.tensor_scalar(
                se_[par][:, :], se_[par][:, :], 1.0, 1.0, ALU.mult, ALU.add))
            EM("vector", lambda E, par=par: E.reciprocal(
                se_[par][:, :], se_[par][:, :]))
            if xs2f_free[par] is not None:
                W("vector", swb, xs2f_free[par])
            se_free[par] = EM("vector", lambda E, par=par: E.tensor_mul(
                xs2f[par][:, :], cv1[par][:, :], se_[par][:, :]), sv)
            if m == 0 and xs2b_guard[0] is not None:
                W("vector", spe, xs2b_guard[0])
            v_cast = EM("vector", lambda E, m=m, par=par: E.tensor_copy(
                xs2b[:, m * TH:(m + 1) * TH], xs2f[par][:, :]), sv)
            W("sync", sv, se_free[par])
            xs2f_free[par] = EM(
                "sync", lambda E, m=m, par=par, hI=hI: E.dma_start(
                    out=xs2_sc.ap()[m * 128:(m + 1) * 128,
                                    hI * TH:(hI + 1) * TH],
                    in_=xs2f[par][:, :]), swb, 16)
            return v_cast

        def emit_ln(hI, m, par, pd_start_guard):
            if spb_free[par] is not None:
                W("scalar", spe, spb_free[par])
            v_l = EM("scalar", lambda E, par=par: E.activation(
                spb[par][:, :], spe_[par][:, :], AF.Ln, bias=cst[:, 1:2]), ssc)
            W("tensor", ssc, v_l)
            if m == 0 and pd_start_guard is not None:
                W("tensor", ssc, pd_start_guard)
            spb_free[par] = EM("tensor", lambda E, m=m, par=par: E.matmul(
                PD[:, 0:TH], onesb[:, :], spb[par][:, 0:TH],
                start=(m == 0), stop=(m == KC - 1)), spe)

        v_xp = EM("sync", lambda E: E.dma_start(
            out=x_half[:, 0:D], in_=xprev_in), sld, 16)
        W("vector", sld, v_xp)
        emit_stats(x_half, 1, None, True)
        if (debug or 0) >= 2:
            for i, t in enumerate((lv1, rs12t, varc, r2t, vzt)):
                EM("vector", lambda E, i=i, t=t: E.tensor_copy(
                    snap3[:, 59 + i:60 + i], t[:, 0:1]))
        mk_xfree = emit_z(x_half, zhalf, 1)
        emit_transposes(zhalf, hp_fm, 1, 128, True)
        mk_zfree = spe.n
        winb_free = [None, None]
        for m in range(KC):
            par = m % 2
            if winb_free[par] is not None:
                W("sync", spe, winb_free[par])
            v_wl = EM("sync", lambda E, m=m, par=par: E.dma_start(
                out=winb[par][:, :].rearrange("p (k c) -> p k c", c=128),
                in_=win_bf.ap().rearrange(
                    "(k p) (m c) -> m p k c", p=128, c=128)[m]), swin, 16)
            W("tensor", swin, v_wl)
            if pw_free[par] is not None:
                W("tensor", ssc, pw_free[par])
            for k in range(KD):
                EM("tensor", lambda E, m=m, k=k, par=par: E.matmul(
                    PW[par][:, 0:128],
                    winb[par][:, k * 128:(k + 1) * 128],
                    hp_fm[:, k * 128:(k + 1) * 128],
                    start=(k == 0), stop=(k == KD - 1)),
                   spe if k == KD - 1 else None)
            winb_free[par] = spe.n
            W("scalar", spe, spe.n)
            v_h3 = EM("scalar", lambda E, par=par: E.copy(
                htmp3[:, :], PW[par][:, 125:128]), ssc)
            pw_free[par] = v_h3
            W("vector", ssc, v_h3)
            vm = EM("vector", lambda E, m=m: E.tensor_scalar(
                halo[:, 3 * m:3 * m + 3], htmp3[:, :], hmv[:, 0:1], None,
                ALU.mult), sv)
            if m + 1 < KC:
                W("scalar", sv, vm)

        if (debug or 0) >= 2:
            NOP("vector", 2000)
            EM("vector", lambda E: E.tensor_copy(snap3[:, 0:48], halo[:, :]))
            EM("vector", lambda E: E.tensor_copy(snap3[:, 48:51], htmp3[:, :]))
            EM("vector", lambda E: E.tensor_copy(
                snap3[:, 51:59], hp_fm[:, 0:8]))
        mk_hfm_free = None
        pd_free = None
        ps_free = None
        v_dA_last = None

        for hI in range(NH):
            W("sync", sv, mk_xfree)
            v_x = EM("sync", lambda E, hI=hI: E.dma_start(
                out=x_half[:, :].rearrange("p (g d) -> p g d", d=D),
                in_=x_in.rearrange("(hh g p) d -> hh p g d", g=4, p=128)[hI]),
                sld, 16)
            W("vector", sld, v_x)
            W("vector", spe, mk_zfree)
            if (debug or 0) >= 2 and hI == 0:
                W("gpsimd", sld, v_x)
                EM("gpsimd", lambda E: E.dma_start(
                    out=dbg["d_xh"], in_=x_half[:, :]), swb, 16)
            emit_stats(x_half, 4, hI * 4, True)
            if (debug or 0) >= 2 and hI == 0:
                W("gpsimd", sv, sv.n)
                W("gpsimd", ssc, ssc.n)
                EM("gpsimd", lambda E: E.dma_start(
                    out=dbg["d_mv"][:, 0:8], in_=mvt[:, :]), swb, 16)
                EM("gpsimd", lambda E: E.dma_start(
                    out=dbg["d_mv"][:, 8:12], in_=rs12t[:, :]), swb, 16)
                EM("gpsimd", lambda E: E.dma_start(
                    out=dbg["d_mv"][:, 12:16], in_=rs1t[:, :]), swb, 16)
                EM("gpsimd", lambda E: E.dma_start(
                    out=dbg["d_mv"][:, 16:20], in_=lv1[:, :]), swb, 16)
                EM("gpsimd", lambda E: E.dma_start(
                    out=dbg["d_mv"][:, 20:24], in_=lv2[:, :]), swb, 16)
                EM("gpsimd", lambda E: E.dma_start(
                    out=dbg["d_mv"][:, 24:28], in_=varc[:, :]), swb, 16)
                EM("gpsimd", lambda E: E.dma_start(
                    out=dbg["d_mv"][:, 28:32], in_=muc[:, :]), swb, 16)
            if (debug or 0) >= 2 and hI == 0:
                for i, t in enumerate((varc, muc, lv1, r2t, rs1t, vzt, lv2,
                                       s12v, rs12t, nb12t)):
                    EM("vector", lambda E, i=i, t=t: E.tensor_copy(
                        snap[:, 4 * i:4 * i + 4], t[:, 0:4]))
            mk_xfree = emit_z(x_half, zhalf, 4)
            if (debug or 0) >= 2 and hI == 0:
                W("gpsimd", sv, mk_xfree)
                EM("gpsimd", lambda E: E.dma_start(
                    out=dbg["d_z"], in_=zhalf[:, :]), swb, 16)
            if mk_hfm_free is not None:
                W("scalar", spe, mk_hfm_free)
            emit_transposes(zhalf, h_fm, 4, TH, True)
            mk_zfree = spe.n
            if (debug or 0) >= 2 and hI == 0:
                W("gpsimd", ssc, ssc.n)
                vgh = EM("gpsimd", lambda E: E.dma_start(
                    out=dbg["d_h"], in_=h_fm[:, :]), swb, 16)
                W("tensor", swb, vgh)

            v_hev = ssc.n
            W("tensor", ssc, v_hev)
            conv_prev = None
            for m in range(32):
                par = m % 2
                if winb_free[par] is not None:
                    W("sync", spe, winb_free[par])
                v_wl = EM("sync", lambda E, m=m, par=par: E.dma_start(
                    out=winb[par][:, :].rearrange("p (k c) -> p k c", c=128),
                    in_=win_bf.ap().rearrange(
                        "(k p) (m c) -> m p k c", p=128, c=128)[m]), swin, 16)
                W("tensor", swin, v_wl)
                if pw_free[par] is not None:
                    W("tensor", ssc, pw_free[par])
                for k in range(KD):
                    EM("tensor", lambda E, m=m, k=k, par=par: E.matmul(
                        PW[par][:, 0:TH],
                        winb[par][:, k * 128:(k + 1) * 128],
                        h_fm[:, k * TH:(k + 1) * TH],
                        start=(k == 0), stop=(k == KD - 1)),
                       spe if k == KD - 1 else None)
                winb_free[par] = spe.n
                v_mm = spe.n
                if m < 16:
                    W("scalar", spe, v_mm)
                    if xse_free[par] is not None:
                        W("scalar", sv, xse_free[par])
                    v_ev = EM("scalar", lambda E, par=par: E.activation(
                        xse[par][:, 3:3 + TH], PW[par][:, 0:TH], AF.Identity),
                        ssc)
                    pw_free[par] = v_ev
                    W("vector", ssc, v_ev)
                    EM("vector", lambda E, m=m, par=par: E.tensor_copy(
                        xse[par][:, 0:3], halo[:, 3 * m:3 * m + 3]))
                    if conv_prev is not None:
                        emit_silu(*conv_prev)
                        conv_prev = None
                    else:
                        NOP("vector", 400)
                    EM("vector", lambda E, m=m, par=par: E.tensor_scalar(
                        cv0[par][:, :], xse[par][:, 0:TH],
                        cwv[:, m:m + 1], cbv[:, m:m + 1], ALU.mult, ALU.add))
                    if cv1_free[par] is not None:
                        W("vector", ssc, cv1_free[par])
                    EM("vector", lambda E, m=m, par=par:
                       E.scalar_tensor_tensor(
                           cv1[par][:, :], xse[par][:, 1:1 + TH],
                           cwv[:, 16 + m:16 + m + 1], cv0[par][:, :],
                           ALU.mult, ALU.add))
                    EM("vector", lambda E, m=m, par=par:
                       E.scalar_tensor_tensor(
                           cv0[par][:, :], xse[par][:, 2:2 + TH],
                           cwv[:, 32 + m:32 + m + 1], cv1[par][:, :],
                           ALU.mult, ALU.add))
                    v_cv = EM("vector", lambda E, m=m, par=par:
                              E.scalar_tensor_tensor(
                                  cv1[par][:, :], xse[par][:, 3:3 + TH],
                                  cwv[:, 48 + m:48 + m + 1], cv0[par][:, :],
                                  ALU.mult, ALU.add), sv)
                    xse_free[par] = EM("vector", lambda E, m=m, par=par:
                                       E.tensor_copy(
                                           halo[:, 3 * m:3 * m + 3],
                                           xse[par][:, TH:TH + 3]), sv)
                    conv_prev = (hI, m, par, v_cv)
                    if debug and hI == 0 and m == 3:
                        W("gpsimd", sv, sv.n)
                        W("gpsimd", ssc, ssc.n)
                        EM("gpsimd", lambda E: E.dma_start(
                            out=dbg["d_sl"][:, 0:TH], in_=xse[0][:, 3:3 + TH]),
                           swb, 16)
                        EM("gpsimd", lambda E: E.dma_start(
                            out=dbg["d_sl"][:, TH:2 * TH], in_=cv1[0][:, :]),
                           swb, 16)
                        EM("gpsimd", lambda E: E.dma_start(
                            out=dbg["d_sl"][:, 2 * TH:3 * TH], in_=se_[0][:, :]),
                           swb, 16)
                        EM("gpsimd", lambda E: E.dma_start(
                            out=dbg["d_sl"][:, 3 * TH:4 * TH],
                            in_=xs2f[0][:, :]), swb, 16)
                else:
                    W("scalar", spe, v_mm)
                    if gatef_free[par] is not None:
                        W("scalar", swb, gatef_free[par])
                    v_ge = EM("scalar", lambda E, par=par: E.activation(
                        gatef[par][:, :], PW[par][:, 0:TH], AF.Identity), ssc)
                    pw_free[par] = v_ge
                    W("sync", ssc, v_ge)
                    gatef_free[par] = EM(
                        "sync", lambda E, m=m, par=par, hI=hI: E.dma_start(
                            out=gate_sc.ap()[(m - 16) * 128:(m - 15) * 128,
                                             hI * TH:(hI + 1) * TH],
                            in_=gatef[par][:, :]), swb, 16)
                    if conv_prev is not None:
                        emit_silu(*conv_prev)
                        conv_prev = None
            mk_hfm_free = spe.n

            v_cast_all = sv.n
            W("tensor", sv, v_cast_all)
            ln_prev = None
            for m in range(KC):
                par = m % 2
                if wdb_free[par] is not None:
                    W("sync", spe, wdb_free[par])
                v_wl = EM("sync", lambda E, m=m, par=par: E.dma_start(
                    out=wdb[par][:, :].rearrange("p (k c) -> p k c", c=128),
                    in_=wd_bf.ap().rearrange(
                        "(k p) (m c) -> m p k c", p=128, c=128)[m]), swd, 16)
                W("tensor", swd, v_wl)
                if pw_free[par] is not None:
                    W("tensor", ssc, pw_free[par])
                for k in range(KC):
                    EM("tensor", lambda E, m=m, k=k, par=par: E.matmul(
                        PW[par][:, 0:TH],
                        wdb[par][:, k * 128:(k + 1) * 128],
                        xs2b[:, k * TH:(k + 1) * TH],
                        start=(k == 0), stop=(k == KC - 1)),
                       spe if k == KC - 1 else None)
                wdb_free[par] = spe.n
                W("scalar", spe, spe.n)
                v_e = EM("scalar", lambda E, m=m, par=par: E.activation(
                    spe_[par][:, :], PW[par][:, 0:TH], AF.Exp,
                    bias=bdv[:, m:m + 1]), ssc)
                pw_free[par] = v_e
                if ln_prev is not None:
                    emit_ln(*ln_prev)
                ln_prev = (hI, m, par, pd_free)
            emit_ln(*ln_prev)

            if (debug or 0) >= 2 and hI == 0:
                W("vector", ssc, ssc.n)
                EM("vector", lambda E: E.tensor_copy(
                    snap2[:, 0:16], xs2b[:, 0:16]))
                EM("vector", lambda E: E.tensor_copy(
                    snap2[:, 16:32], spe_[1][:, 0:16]))
                EM("vector", lambda E: E.tensor_copy(
                    snap2[:, 32:48], spb[1][:, 0:16]))
                EM("vector", lambda E: E.tensor_copy(
                    snap2[:, 48:64], wdb[1][:, 0:16]))
            W("scalar", spe, spe.n)
            v_de = EM("scalar", lambda E: E.activation(
                delta_r[:, :], PD[:, 0:TH], AF.Identity, scale=1.0 / DI), ssc)
            pd_free = v_de
            if (debug or 0) >= 2:
                W("sync", ssc, v_de)
                W("gpsimd", ssc, v_de)
                EM("sync", lambda E, hI=hI: E.dma_start(
                    out=dbg["d_delta"][:, hI * TH:(hI + 1) * TH],
                    in_=delta_r[:, :]), swb, 16)
                EM("gpsimd", lambda E, hI=hI: E.dma_start(
                    out=dbg["d_sp"][:, hI * TH:(hI + 1) * TH],
                    in_=spb[1][:, :]), swb, 16)
            W("tensor", ssc, v_de)
            if ps_free is not None:
                W("tensor", ssc, ps_free)
            EM("tensor", lambda E: E.matmul(
                PS[0:NS, 0:TH], ones116[:, :], delta_r[:, :],
                start=True, stop=True), spe)
            W("scalar", spe, spe.n)
            v_d16 = EM("scalar", lambda E: E.copy(
                d16t[:, 0:TH], PS[0:NS, 0:TH]), ssc)
            ps_free = v_d16
            NOP("scalar", 1000)
            v_dA_last = EM("scalar", lambda E, hI=hI: E.activation(
                dAt[:, hI * TH:(hI + 1) * TH], d16t[:, 0:TH], AF.Exp,
                scale=acol[:, 0:1]), ssc)

            W("tensor", ssc, v_d16)
            for k in range(KC):
                EM("tensor", lambda E, k=k: E.matmul(
                    PS[0:NS, 0:TH], BT[:, k * NS:(k + 1) * NS],
                    xs2b[:, k * TH:(k + 1) * TH],
                    start=(k == 0), stop=(k == KC - 1)),
                   spe if k == KC - 1 else None)
            xs2b_guard[0] = spe.n
            W("scalar", spe, spe.n)
            v_bx = EM("scalar", lambda E: E.copy(
                bxt[:, 0:TH], PS[0:NS, 0:TH]), ssc)
            ps_free = v_bx
            W("vector", ssc, v_bx)
            EM("vector", lambda E, hI=hI: E.tensor_mul(
                Bxt[:, hI * TH:(hI + 1) * TH], bxt[:, 0:TH], d16t[:, 0:TH]),
               sv)

        barrier("b2")
        cur[0] = arena_base
        HSb = at("HSb", [NS, NTOK], BF16)
        p3_base = cur[0]
        HS = at("HS", [NS, NTOK])

        v_sc1 = EM("vector", lambda E: E.tensor_tensor_scan(
            HS[:, :], dAt[:, :], Bxt[:, :], 0.0, ALU.mult, ALU.add), sv)
        W("tensor", sv, v_sc1)
        v_tr = EM("tensor", lambda E: E.transpose(
            PS[0:1, 0:NS], HS[:, NTOK - 1:NTOK], idf[0:NS, 0:NS]), spe)
        W("scalar", spe, v_tr)
        v_hr = EM("scalar", lambda E: E.copy(hrow[:, :], PS[0:1, 0:NS]), ssc)
        W("sync", ssc, v_hr)
        v_ci = EM("sync", lambda E: E.dma_start(out=cc_i.ap(), in_=hrow[:, :]),
                  swb, 16)
        W("gpsimd", swb, v_ci)
        v_cc = EM("gpsimd", lambda E: E.collective_compute(
            "AllGather", ALU.bypass, replica_groups=[list(range(8))],
            ins=[cc_i.ap()], outs=[cc_o.ap()]), scc)
        W("sync", scc, v_cc)
        v_g = EM("sync", lambda E: E.dma_start(out=Gsb[:, :], in_=cc_o.ap()),
                 sld, 16)
        W("tensor", sld, v_g)
        v_sel = EM("tensor", lambda E: E.matmul(
            PS[0:NS, 512:513], Gsb[:, :], selv[:, :], start=True, stop=True),
            spe)
        W("scalar", spe, v_sel)
        v_h0 = EM("scalar", lambda E: E.copy(h0sb[:, :], PS[0:NS, 512:513]),
                  ssc)
        W("vector", ssc, v_h0)
        W("vector", spe, v_tr)
        EM("vector", lambda E: E.tensor_tensor_scan(
            HS[:, :], dAt[:, :], Bxt[:, :], h0sb[:, 0:1], ALU.mult, ALU.add),
            sv)
        v_hsb = EM("vector", lambda E: E.tensor_copy(HSb[:, :], HS[:, :]), sv)
        if (debug or 0) >= 2:
            W("sync", sv, v_hsb)
            W("sync", swb, swb.n)
            EM("sync", lambda E: E.dma_start(
                out=dbg["d_mv"][:, 0:32], in_=snap[:, 0:32]), swb, 16)
            EM("sync", lambda E: E.dma_start(
                out=dbg["d_sl"][:, 0:32], in_=snap[:, 32:64]), swb, 16)
            EM("sync", lambda E: E.dma_start(
                out=dbg["d_sl"][:, 32:96], in_=snap2[:, 0:64]), swb, 16)
            EM("sync", lambda E: E.dma_start(
                out=dbg["d_sl"][:, 96:160], in_=snap3[:, 0:64]), swb, 16)
            EM("sync", lambda E: E.dma_start(out=dbg["d_hs"], in_=HS[:, :]),
               swb, 16)
            EM("sync", lambda E: E.dma_start(out=dbg["d_da"], in_=dAt[:, :]),
               swb, 16)
            EM("sync", lambda E: E.dma_start(out=dbg["d_bx"], in_=Bxt[:, :]),
               swb, 16)
            EM("sync", lambda E: E.dma_start(out=dbg["d_xs2"], in_=xs2_sc.ap()),
               swb, 16)
            EM("sync", lambda E: E.dma_start(out=dbg["d_gate"], in_=gate_sc.ap()),
               swb, 16)

        barrier("b3", real={"vector": (sv, v_hsb)})
        W("sync", swb, swb.n)

        cur[0] = p3_base
        woutb = at("woutb", [128, KC * D], BF16)
        xq = at("xq", [128, 2 * D])
        x1q = at("x1q", [128, 2 * D])
        zq = at("zq", [128, 2 * D], BF16)
        z3f = at("z3f", [128, KD * QT], BF16)
        yb = at("yb", [128, KC * QT], BF16)
        xs2r = [at(f"xs2r{i}", [128, QT]) for i in range(2)]
        gater = [at(f"gater{i}", [128, QT]) for i in range(2)]
        yraw = [at(f"yraw{i}", [128, QT]) for i in range(2)]
        ytmp = [at(f"ytmp{i}", [128, QT]) for i in range(2)]
        gex = [at(f"gex{i}", [128, QT]) for i in range(2)]
        atile = [at(f"atile{i}", [128, D]) for i in range(2)]
        assert cur[0] <= arena_top

        v_wo = EM("sync", lambda E: E.dma_start(
            out=woutb[:, :].rearrange("p (k d) -> p k d", d=D),
            in_=wo_bf.ap().rearrange("(k p) d -> p k d", p=128)), sld, 16)

        mk_xq_free = None
        mk_x1_free = None
        mk_zq_free = None
        mk_z3f_free = None
        mk_yb_free = None
        xs2r_free = [None, None]
        gater_free = [None, None]
        yraw_free = [None, None]
        ytmp_free = [None, None]
        gex_free = [None, None]
        atile_free = [None, None]
        pw_free = [None, None]
        sil_prev = [None]

        def emit_ysilu(q, m, par, v_gx):
            W("vector", ssc, v_gx)
            EM("vector", lambda E, par=par: E.tensor_scalar(
                gex[par][:, :], gex[par][:, :], 1.0, 1.0, ALU.mult, ALU.add))
            EM("vector", lambda E, par=par: E.reciprocal(
                gex[par][:, :], gex[par][:, :]))
            gater_free[par] = EM("vector", lambda E, par=par: E.tensor_mul(
                ytmp[par][:, :], ytmp[par][:, :], gater[par][:, :]), sv)
            v_y3 = EM("vector", lambda E, m=m, par=par: E.tensor_mul(
                yb[:, m * QT:(m + 1) * QT], ytmp[par][:, :], gex[par][:, :]),
                sv)
            gex_free[par] = v_y3
            ytmp_free[par] = v_y3

        W("tensor", sv, v_hsb)
        W("tensor", sld, v_wo)
        for q in range(NQ):
            if mk_xq_free is not None:
                W("sync", sv, mk_xq_free)
            v_x = EM("sync", lambda E, q=q: E.dma_start(
                out=xq[:, :].rearrange("p (g d) -> p g d", d=D),
                in_=x_in.rearrange("(q g p) d -> q p g d", g=2, p=128)[q]),
                sld, 16)
            if mk_yb_free is not None:
                W("vector", spe, mk_yb_free)
            for m in range(KC):
                par = m % 2
                if pw_free[par] is not None:
                    W("tensor", ssc, pw_free[par])
                EM("tensor", lambda E, m=m, q=q, par=par: E.matmul(
                    PW[par][:, 0:QT], CT[:, m * 128:(m + 1) * 128],
                    HSb[:, q * QT:(q + 1) * QT], start=True, stop=True), spe)
                v_c = spe.n
                if xs2r_free[par] is not None:
                    W("sync", sv, xs2r_free[par])
                v_lx = EM("sync", lambda E, m=m, q=q, par=par: E.dma_start(
                    out=xs2r[par][:, :],
                    in_=xs2_sc.ap()[m * 128:(m + 1) * 128,
                                    q * QT:(q + 1) * QT]), srr, 16)
                if gater_free[par] is not None:
                    W("sync", sv, gater_free[par])
                v_lg = EM("sync", lambda E, m=m, q=q, par=par: E.dma_start(
                    out=gater[par][:, :],
                    in_=gate_sc.ap()[m * 128:(m + 1) * 128,
                                     q * QT:(q + 1) * QT]), srr, 16)
                W("scalar", spe, v_c)
                if yraw_free[par] is not None:
                    W("scalar", sv, yraw_free[par])
                v_yr = EM("scalar", lambda E, par=par: E.activation(
                    yraw[par][:, :], PW[par][:, 0:QT], AF.Identity), ssc)
                pw_free[par] = v_yr
                W("scalar", srr, v_lg)
                if gex_free[par] is not None:
                    W("scalar", sv, gex_free[par])
                v_gx = EM("scalar", lambda E, par=par: E.activation(
                    gex[par][:, :], gater[par][:, :], AF.Exp, scale=-1.0),
                    ssc)
                W("vector", srr, v_lx)
                W("vector", ssc, v_yr)
                if ytmp_free[par] is not None:
                    W("vector", sv, ytmp_free[par])
                yraw_free[par] = EM(
                    "vector", lambda E, m=m, par=par: E.scalar_tensor_tensor(
                        ytmp[par][:, :], xs2r[par][:, :], dvv[:, m:m + 1],
                        yraw[par][:, :], ALU.mult, ALU.add), sv)
                xs2r_free[par] = yraw_free[par]
                if sil_prev[0] is not None:
                    emit_ysilu(*sil_prev[0])
                sil_prev[0] = (q, m, par, v_gx)
            emit_ysilu(*sil_prev[0])
            sil_prev[0] = None
            mk_yb_done = sv.n

            W("vector", sld, v_x)
            if mk_x1_free is not None:
                W("vector", swb, mk_x1_free)
            for tq in range(2):
                EM("vector", lambda E, q=q, tq=tq: E.tensor_scalar(
                    x1q[:, tq * D:(tq + 1) * D], xq[:, tq * D:(tq + 1) * D],
                    ps1a[:, 2 * q + tq:2 * q + tq + 1],
                    pb1a[:, 2 * q + tq:2 * q + tq + 1], ALU.mult, ALU.add),
                   sv if tq == 1 else None)
            mk_xq_free = sv.n
            W("tensor", sv, mk_yb_done)
            for tq in range(2):
                par = tq
                if pw_free[par] is not None:
                    W("tensor", ssc, pw_free[par])
                for nh in range(2):
                    for k in range(KC):
                        EM("tensor", lambda E, tq=tq, nh=nh, k=k, par=par:
                           E.matmul(
                               PW[par][:, nh * 512:(nh + 1) * 512],
                               yb[:, k * QT + tq * 128:
                                  k * QT + (tq + 1) * 128],
                               woutb[:, k * D + nh * 512:
                                     k * D + (nh + 1) * 512],
                               start=(k == 0), stop=(k == KC - 1)),
                           spe if (k == KC - 1 and nh == 1) else None)
                W("scalar", spe, spe.n)
                if atile_free[par] is not None:
                    W("scalar", sv, atile_free[par])
                v_at = EM("scalar", lambda E, par=par: E.activation(
                    atile[par][:, :], PW[par][:, 0:D], AF.Identity), ssc)
                pw_free[par] = v_at
                W("vector", ssc, v_at)
                atile_free[par] = EM(
                    "vector", lambda E, tq=tq, par=par: E.tensor_add(
                        x1q[:, tq * D:(tq + 1) * D],
                        x1q[:, tq * D:(tq + 1) * D], atile[par][:, :]), sv)
            mk_yb_free = spe.n

            emit_stats(x1q, 2, None, False)
            if (debug or 0) >= 2 and q == 0:
                for i, t in enumerate((varc, muc, lv1, rs12t, nb12t)):
                    EM("vector", lambda E, i=i, t=t: E.tensor_copy(
                        snap4[:, 4 * i:4 * i + 4], t[:, 0:4]))
            if mk_zq_free is not None:
                W("vector", spe, mk_zq_free)
            v_z3 = emit_z(x1q, zq, 2)
            if (debug or 0) >= 2 and q == 0:
                EM("vector", lambda E: E.tensor_copy(
                    snap4[:, 32:96], zq[:, 0:64]))
                EM("vector", lambda E: E.tensor_copy(
                    snap4[:, 96:160], x1q[:, 0:64]))
            W("sync", sv, v_z3)
            mk_x1_free = EM("sync", lambda E, q=q: E.dma_start(
                out=x1_sc.ap()[q * QT:(q + 1) * QT, :].rearrange(
                    "(g p) d -> p g d", p=128),
                in_=x1q[:, :].rearrange("p (g d) -> p g d", d=D)), swb, 16)
            if debug:
                EM("sync", lambda E, q=q: E.dma_start(
                    out=dbg["d_x1"][q * QT:(q + 1) * QT, :].rearrange(
                        "(g p) d -> p g d", p=128),
                    in_=x1q[:, :].rearrange("p (g d) -> p g d", d=D)),
                   swb, 16)
            if mk_z3f_free is not None:
                W("tensor", swb, mk_z3f_free)
            W("tensor", sv, v_z3)
            for j in range(KD):
                par = j % 2
                if pw_free[par] is not None:
                    W("tensor", ssc, pw_free[par])
                for tq in range(2):
                    EM("tensor", lambda E, j=j, tq=tq, par=par: E.matmul(
                        PW[par][:, tq * 128:(tq + 1) * 128],
                        zq[:, tq * D + j * 128:tq * D + (j + 1) * 128],
                        idb[:, :], start=True, stop=True),
                       spe if tq == 1 else None)
                W("scalar", spe, spe.n)
                pw_free[par] = EM(
                    "scalar", lambda E, j=j, par=par: E.activation(
                        z3f[:, j * QT:(j + 1) * QT], PW[par][:, 0:QT],
                        AF.Identity), ssc)
            mk_zq_free = spe.n
            W("sync", ssc, ssc.n)
            mk_z3f_free = EM("sync", lambda E, q=q: E.dma_start(
                out=z3_sc.ap().rearrange("(j p) n -> p j n", p=128)[
                    :, :, q * QT:(q + 1) * QT],
                in_=z3f[:, :].rearrange("p (j t) -> p j t", t=QT)), swb, 16)

        barrier("b4")
        W("sync", swb, swb.n)
        cur[0] = arena_base
        s2b = at("s2b", [128, KH * D], BF16)
        gq = at("gq", [128, KH * QT], BF16)
        z3r = [at(f"z3r{i}", [128, KD * QT], BF16) for i in range(2)]
        x1r = [at(f"x1r{i}", [128, 2 * D]) for i in range(2)]
        s1b = [at(f"s1b{i}", [128, KD * 128], BF16) for i in range(2)]
        eqb = [at(f"eqb{i}", [128, QT]) for i in range(2)]
        uqb = [at(f"uqb{i}", [128, QT]) for i in range(2)]
        e1b = [at(f"e1b{i}", [128, QT]) for i in range(2)]
        otile = [at(f"otile{i}", [128, D]) for i in range(2)]
        outq = [at(f"outq{i}", [128, D]) for i in range(2)]
        assert cur[0] <= arena_top

        load_act_set(2)

        v_s2 = EM("sync", lambda E: E.dma_start(
            out=s2b[:, :].rearrange("p (k d) -> p k d", d=D),
            in_=s2t_bf.ap().rearrange("(k p) d -> p k d", p=128)), sld, 16)

        z3r_free = [None, None]
        x1r_free = [None, None]
        s1b_free = [None, None]
        eq_free = [None, None]
        uq_free = [None, None]
        e1_free = [None, None]
        otile_free = [None, None]
        outq_free = [None, None]
        pw_free = [None, None]
        g_prev = [None]
        mk_gq_free = None

        def emit_g(q, m, par):
            uq_free[par] = EM("vector", lambda E, m=m, par=par: E.tensor_mul(
                gq[:, m * QT:(m + 1) * QT], uqb[par][:, :], e1b[par][:, :]),
                sv)
            e1_free[par] = uq_free[par]

        for q in range(NQ):
            parq = q % 2
            if z3r_free[parq] is not None:
                W("sync", spe, z3r_free[parq])
            v_z3l = EM("sync", lambda E, q=q, parq=parq: E.dma_start(
                out=z3r[parq][:, :].rearrange("p (j t) -> p j t", t=QT),
                in_=z3_sc.ap().rearrange("(j p) n -> p j n", p=128)[
                    :, :, q * QT:(q + 1) * QT]), sx3, 16)
            if x1r_free[parq] is not None:
                W("sync", sv, x1r_free[parq])
            v_x1l = EM("sync", lambda E, q=q, parq=parq: E.dma_start(
                out=x1r[parq][:, :].rearrange("p (g d) -> p g d", d=D),
                in_=x1_sc.ap()[q * QT:(q + 1) * QT, :].rearrange(
                    "(g p) d -> p g d", p=128)), sx3, 16)
            W("tensor", sx3, v_z3l)
            if mk_gq_free is not None:
                W("vector", spe, mk_gq_free)
            for m in range(KH):
                par = m % 2
                if s1b_free[par] is not None:
                    W("sync", spe, s1b_free[par])
                v_s1 = EM("sync", lambda E, m=m, par=par: E.dma_start(
                    out=s1b[par][:, :].rearrange("p (k c) -> p k c", c=128),
                    in_=s1t_bf.ap().rearrange(
                        "(k p) (m c) -> m p k c", p=128, c=128)[m]), swin, 16)
                W("tensor", swin, v_s1)
                if pw_free[par] is not None:
                    W("tensor", ssc, pw_free[par])
                for k in range(KD):
                    EM("tensor", lambda E, m=m, k=k, par=par, parq=parq:
                       E.matmul(PW[par][:, 0:QT],
                                s1b[par][:, k * 128:(k + 1) * 128],
                                z3r[parq][:, k * QT:(k + 1) * QT],
                                start=(k == 0), stop=(k == KD - 1)),
                       spe if k == KD - 1 else None)
                s1b_free[par] = spe.n
                W("scalar", spe, spe.n)
                if eq_free[par] is not None:
                    W("scalar", sv, eq_free[par])
                EM("scalar", lambda E, m=m, par=par: E.activation(
                    eqb[par][:, :], PW[par][:, 0:QT], AF.Erf,
                    scale=a1s[:, m:m + 1], bias=b1s[:, m:m + 1]))
                if uq_free[par] is not None:
                    W("scalar", sv, uq_free[par])
                v_u = EM("scalar", lambda E, m=m, par=par: E.activation(
                    uqb[par][:, :], PW[par][:, 0:QT], AF.Identity,
                    scale=a1v[:, m:m + 1], bias=b1v[:, m:m + 1]), ssc)
                pw_free[par] = v_u
                W("vector", ssc, v_u)
                if e1_free[par] is not None:
                    W("vector", sv, e1_free[par])
                eq_free[par] = EM("vector", lambda E, par=par: E.tensor_scalar(
                    e1b[par][:, :], eqb[par][:, :], 0.5, 0.5,
                    ALU.mult, ALU.add), sv)
                if g_prev[0] is not None:
                    emit_g(*g_prev[0])
                g_prev[0] = (q, m, par)
            emit_g(*g_prev[0])
            g_prev[0] = None
            mk_g_done = sv.n
            z3r_free[parq] = spe.n

            W("tensor", sv, mk_g_done)
            if q == 0:
                W("tensor", sld, v_s2)
            for tq in range(2):
                par = tq
                if pw_free[par] is not None:
                    W("tensor", ssc, pw_free[par])
                for nh in range(2):
                    for k in range(KH):
                        EM("tensor", lambda E, tq=tq, nh=nh, k=k, par=par:
                           E.matmul(
                               PW[par][:, nh * 512:(nh + 1) * 512],
                               gq[:, k * QT + tq * 128:
                                  k * QT + (tq + 1) * 128],
                               s2b[:, k * D + nh * 512:
                                   k * D + (nh + 1) * 512],
                               start=(k == 0), stop=(k == KH - 1)),
                           spe if (k == KH - 1 and nh == 1) else None)
                W("scalar", spe, spe.n)
                if otile_free[par] is not None:
                    W("scalar", sv, otile_free[par])
                v_ot = EM("scalar", lambda E, par=par: E.activation(
                    otile[par][:, :], PW[par][:, 0:D], AF.Identity), ssc)
                pw_free[par] = v_ot
                W("vector", ssc, v_ot)
                W("vector", sx3, v_x1l)
                if outq_free[par] is not None:
                    W("vector", sout, outq_free[par])
                EM("vector", lambda E, par=par: E.tensor_mul(
                    outq[par][:, :], otile[par][:, :], a2bc[:, :]))
                EM("vector", lambda E, par=par: E.tensor_add(
                    outq[par][:, :], outq[par][:, :], b2bc[:, :]))
                v_o3 = EM("vector", lambda E, tq=tq, par=par, parq=parq:
                          E.tensor_add(outq[par][:, :], outq[par][:, :],
                                       x1r[parq][:, tq * D:(tq + 1) * D]), sv)
                otile_free[par] = v_o3
                if tq == 1:
                    x1r_free[parq] = v_o3
                W("sync", sv, v_o3)
                outq_free[par] = EM(
                    "sync", lambda E, q=q, tq=tq, par=par: E.dma_start(
                        out=out_d[q * QT + tq * 128:
                                  q * QT + (tq + 1) * 128, :],
                        in_=outq[par][:, :]), sout, 16)
            mk_gq_free = spe.n

        W("sync", sout, sout.n)
        W("gpsimd", sout, sout.n)
        if debug:
            EM("gpsimd", lambda E: E.dma_start(out=dbg["d_s1"],
                                               in_=s1t_bf.ap()), swb, 16)
            EM("gpsimd", lambda E: E.dma_start(out=dbg["d_s2"],
                                               in_=s2t_bf.ap()), swb, 16)
            EM("gpsimd", lambda E: E.dma_start(out=dbg["d_z3"],
                                               in_=z3_sc.ap()), swb, 16)
            EM("gpsimd", lambda E: E.dma_start(out=dbg["d_a"][:, 0:KH],
                                               in_=a1v[:, :]), swb, 16)
            EM("gpsimd", lambda E: E.dma_start(
                out=dbg["d_a"][:, KH:2 * KH], in_=a1s[:, :]), swb, 16)
            EM("gpsimd", lambda E: E.dma_start(
                out=dbg["d_a"][:, 2 * KH:3 * KH], in_=b1s[:, :]), swb, 16)
            EM("gpsimd", lambda E: E.dma_start(
                out=dbg["d_a"][:, 3 * KH:3 * KH + D], in_=a2bc[:, :]), swb, 16)
            EM("gpsimd", lambda E: E.dma_start(
                out=dbg["d_a"][:, 3 * KH + D:3 * KH + 2 * D],
                in_=b2bc[:, :]), swb, 16)
            EM("gpsimd", lambda E: E.dma_start(
                out=dbg["d_mv"][:, 0:32], in_=snap4[:, 0:32]), swb, 16)
            EM("gpsimd", lambda E: E.dma_start(
                out=dbg["d_sl"][:, 0:160], in_=snap4[:, 32:192]), swb, 16)
            W("gpsimd", swb, swb.n)

        with nc.Block() as blk:
            blk.sync(lambda E: [f(E) for f in prog["sync"]])
            blk.tensor(lambda E: [f(E) for f in prog["tensor"]])
            blk.vector(lambda E: [f(E) for f in prog["vector"]])
            blk.scalar(lambda E: [f(E) for f in prog["scalar"]])
            blk.gpsimd(lambda E: [f(E) for f in prog["gpsimd"]])

    return nc


def kernel(**inputs):
    x = np.ascontiguousarray(np.asarray(inputs["L0_attn_x"], dtype=np.float32))
    B, L, _ = x.shape
    gam = np.asarray(inputs["L0_attn_ln_gamma"], np.float32)
    bet = np.asarray(inputs["L0_attn_ln_beta"], np.float32)
    W_in = np.ascontiguousarray(np.asarray(inputs["L0_attn_W_in"], np.float32))
    conv_w = np.asarray(inputs["L0_attn_conv_w"], np.float32)
    conv_b = np.asarray(inputs["L0_attn_conv_b"], np.float32)
    A_log = np.asarray(inputs["L0_attn_A_log"], np.float32)
    B_mat = np.ascontiguousarray(np.asarray(inputs["L0_attn_B_mat"], np.float32))
    C_mat = np.ascontiguousarray(np.asarray(inputs["L0_attn_C_mat"], np.float32))
    D_vec = np.asarray(inputs["L0_attn_D_vec"], np.float32)
    Wd = np.ascontiguousarray(np.asarray(inputs["L0_attn_Wd"], np.float32))
    bd = np.asarray(inputs["L0_attn_bd"], np.float32)
    W_out = np.ascontiguousarray(np.asarray(inputs["L0_attn_W_out"], np.float32))
    w1 = np.ascontiguousarray(np.asarray(inputs["L0_ffn_w1"], np.float32))
    b1 = np.asarray(inputs["L0_ffn_b1"], np.float32)
    w2 = np.ascontiguousarray(np.asarray(inputs["L0_ffn_w2"], np.float32))
    b2 = np.asarray(inputs["L0_ffn_b2"], np.float32)

    def pcv(v, nchunk):
        return np.ascontiguousarray(v.reshape(nchunk, 128).T)

    cwT = np.ascontiguousarray(
        conv_w.reshape(KC, 128, 4).transpose(1, 2, 0).reshape(128, 64))
    cst = np.zeros((128, 8), np.float32)
    cst[:, 1] = 1.0
    cst[:, 2] = EPS
    idf = np.eye(128, dtype=np.float32)
    common = {
        "cst": cst, "idf": idf,
        "W_in": W_in, "cwT": cwT, "cbT": pcv(conv_b, KC),
        "A_log": np.ascontiguousarray(A_log.reshape(NS, 1)),
        "B_mat": B_mat, "C_mat": C_mat, "DvT": pcv(D_vec, KC),
        "Wd": Wd, "bdT": pcv(bd, KC), "W_out": W_out,
        "gamT": pcv(gam, KD), "betT": pcv(bet, KD),
        "w1": w1, "b1T": pcv(b1, KH), "w2": w2,
        "b2bc": np.ascontiguousarray(np.broadcast_to(b2, (128, D))),
    }
    xf = x.reshape(B * 2, NTOK, D)
    in_maps = []
    for c in range(8):
        half = c % 2
        xs = np.ascontiguousarray(xf[c])
        if half == 0:
            xprev = np.ascontiguousarray(xs[0:128])
            hm = np.zeros((128, 1), np.float32)
            sel = np.zeros((8, 1), np.float32)
        else:
            xprev = np.ascontiguousarray(xf[c - 1][NTOK - 128:])
            hm = np.ones((128, 1), np.float32)
            sel = np.zeros((8, 1), np.float32)
            sel[c - 1, 0] = 1.0
        im = dict(common)
        im.update({"x": xs, "xprev": xprev, "hmask": hm, "sel": sel})
        in_maps.append(im)

    nc = build_program(debug=DEBUG)
    res = run_bass_kernel_spmd(nc, in_maps, core_ids=list(range(8)))
    out = np.empty((B * 2, NTOK, D), np.float32)
    for c in range(8):
        out[c] = np.asarray(res.results[c]["out"], np.float32)
    if DEBUG:
        kernel.last_results = res
    return out.reshape(B, L, D)
